# version 02
# baseline (speedup 1.0000x reference)
import numpy as np
from contextlib import ExitStack

import concourse.bass as bass
import concourse.mybir as mybir
from concourse.bass_utils import run_bass_kernel_spmd

F32 = mybir.dt.float32
BF16 = mybir.dt.bfloat16
U8 = mybir.dt.uint8
AF = mybir.ActivationFunctionType
ALU = mybir.AluOpType
AX = mybir.AxisListType

NCORES = 8
T = 2048
D = 2048
TC = 16
KC = 16
INTER = 1408
IC = 11
NE = 16
ROPE = 64

CFG = {
    "n_experts": NE,
    "n_shared": 2,
    "phases": ("A", "B", "C"),
    "debug": (),
}


def _rope_tables_np():
    inv = 1.0 / 10000.0 ** (np.arange(0, ROPE, 2) / ROPE)
    fr = np.repeat(np.arange(T)[:, None] * inv[None, :], 2, axis=1)
    C = np.cos(fr).astype(np.float32)
    S = np.sin(fr).astype(np.float32)
    j = np.arange(ROPE)
    SA = np.where(j[None, :] % 2 == 0, -S, 0.0).astype(np.float32)
    SB = np.zeros((T, ROPE), np.float32)
    SB[:, 0:63] = np.where(j[None, 0:63] % 2 == 0, S[:, 1:64], 0.0)
    lay = lambda a: np.ascontiguousarray(a.reshape(TC, 128, ROPE).transpose(1, 0, 2))
    return lay(C), lay(SA), lay(SB)


class Sched:
    def __init__(self, nc, es):
        self.nc, self.es = nc, es
        self.events = []
        self.sems = {}
        self.cnt = {}

    def sem(self, name):
        self.sems[name] = self.es.enter_context(self.nc.semaphore(name))
        self.cnt[name] = 0

    def ev(self, engine, emit, waits=(), inc=None, n=1):
        w = tuple((s, v) for m in waits if m is not None for (s, v) in [m] if v > 0)
        self.events.append((engine, w, emit, inc, n))
        if inc is not None:
            self.cnt[inc] += n
            return (inc, self.cnt[inc])
        return None

    def mark(self, sem):
        return (sem, self.cnt[sem])

    def run(self, eng, name):
        lastw = {}
        for (en, waits, emit, inc, n) in self.events:
            if en != name:
                continue
            for (s, v) in waits:
                if lastw.get(s, -1) >= v:
                    continue
                eng.wait_ge(self.sems[s], v)
                lastw[s] = v
            inst = emit(eng)
            if inc is not None:
                assert inst is not None, "inc event emitted no instruction"
                inst.then_inc(self.sems[inc], n)


def _build(cfg=CFG):
    import ml_dtypes
    nc = bass.Bass("TRN2", target_bir_lowering=False, debug=False)
    es = ExitStack()

    dr = {}
    dr["X"] = nc.dram_tensor("X", [T, D], F32, kind="ExternalInput").ap()
    dr["WQ"] = nc.dram_tensor("WQ", [D, D], F32, kind="ExternalInput").ap()
    dr["GW"] = nc.dram_tensor("GW", [NE, D], F32, kind="ExternalInput").ap()
    dr["W1"] = nc.dram_tensor("W1", [NE, D, INTER], F32, kind="ExternalInput").ap()
    dr["W2"] = nc.dram_tensor("W2", [NE, INTER, D], F32, kind="ExternalInput").ap()
    dr["W3"] = nc.dram_tensor("W3", [NE, D, INTER], F32, kind="ExternalInput").ap()
    dr["SW1"] = nc.dram_tensor("SW1", [D, 2 * INTER], F32, kind="ExternalInput").ap()
    dr["SW2"] = nc.dram_tensor("SW2", [2 * INTER, D], F32, kind="ExternalInput").ap()
    dr["SW3"] = nc.dram_tensor("SW3", [D, 2 * INTER], F32, kind="ExternalInput").ap()
    dr["EB"] = nc.dram_tensor("EB", [128, NE], F32, kind="ExternalInput").ap()
    dr["HHB"] = nc.dram_tensor("HHB", [128, D], F32, kind="ExternalInput").ap()
    dr["GNW"] = nc.dram_tensor("GNW", [128, D], F32, kind="ExternalInput").ap()
    dr["GNB"] = nc.dram_tensor("GNB", [128, D], F32, kind="ExternalInput").ap()
    dr["OUT"] = nc.dram_tensor("OUT", [T, D], F32, kind="ExternalOutput").ap()

    dbg = {}
    for name in cfg["debug"]:
        shp = {"lnxT": [128 * KC, T], "cw": [128, TC * NE], "scores": [128, TC * NE],
               "xph": [T, D], "hT": [128 * IC, 1024]}[name]
        dbg[name] = nc.dram_tensor("DBG_" + name, shp, F32, kind="ExternalOutput").ap()

    ident_np = np.eye(128, dtype=np.float32).astype(ml_dtypes.bfloat16)
    low16_np = np.broadcast_to((np.arange(16)[:, None] < np.arange(16)[None, :])
                               .astype(np.float32).reshape(1, 256), (128, 256)).copy()
    low4_np = np.broadcast_to((np.arange(4)[:, None] < np.arange(4)[None, :])
                              .astype(np.float32).reshape(1, 16), (128, 16)).copy()
    Cn, SAn, SBn = _rope_tables_np()
    bfc = lambda a: a.astype(ml_dtypes.bfloat16)
    d_ident = nc.inline_tensor(ident_np, "c_ident").ap()
    d_low16 = nc.inline_tensor(low16_np, "c_low16").ap()
    d_low4 = nc.inline_tensor(low4_np, "c_low4").ap()
    d_ropeC = nc.inline_tensor(bfc(Cn), "c_ropeC").ap()
    d_ropeSA = nc.inline_tensor(bfc(SAn), "c_ropeSA").ap()
    d_ropeSB = nc.inline_tensor(bfc(SBn), "c_ropeSB").ap()

    sb = lambda n_, s_, d_: es.enter_context(nc.sbuf_tensor(n_, s_, d_))
    xtok = sb("xtok", [128, TC, D], BF16)
    lnxT = sb("lnxT", [128, KC, T], BF16)
    HTR = sb("HTR", [128, 22528], U8)
    STU = sb("STU", [128, 16384], U8)
    GNR = sb("GNR", [128, 13312], U8)
    stw2 = sb("stw2", [128, 2, IC, 256], BF16)
    hhbt = sb("hhbt", [128, D], BF16)
    gnwt = sb("gnwt", [128, D], BF16)
    gnbt = sb("gnbt", [128, D], BF16)
    silub = sb("silub", [128, 2, 512], BF16)
    cw = sb("cw", [128, TC * NE], F32)
    gwT = sb("gwT", [128, KC, NE], BF16)
    identt = sb("identt", [128, 128], BF16)
    low16t = sb("low16t", [128, 256], F32)
    low4t = sb("low4t", [128, 16], F32)
    ebt = sb("ebt", [128, NE], F32)
    ssum = sb("ssum", [128, TC], F32)
    ssq = sb("ssq", [128, TC], F32)
    meanb = sb("meanb", [128, TC], F32)
    msqb = sb("msqb", [128, TC], F32)
    varb = sb("varb", [128, TC], F32)
    rsb = sb("rsb", [128, TC], F32)
    denb = sb("denb", [128, TC], F32)
    recb = sb("recb", [128, TC], F32)

    def view(region, off, nbytes, dt, a=None):
        ap = region[:, off:off + nbytes].bitcast(dt)
        if a is not None:
            ap = ap.rearrange("p (a b) -> p a b", a=a)
        return ap

    hT = view(HTR, 0, IC * 1024 * 2, BF16, a=IC)
    ETB = view(HTR, 0, 1024, F32)
    D1B = view(HTR, 1024, 1024, F32)
    SCO = view(HTR, 2048, 1024, F32)
    BIA = view(HTR, 3072, 1024, F32)
    GSB = view(HTR, 4096, 256, F32)
    CMP4 = view(HTR, 4352, 64, F32)
    TMP4 = view(HTR, 4416, 64, F32)
    U8B = view(HTR, 4480, 32, F32)
    RKG = view(HTR, 4512, 256, F32)
    GMR = view(HTR, 4768, 1024, F32)
    MZB = view(HTR, 5792, 1024, F32)
    MSKB = view(HTR, 6816, 1024, F32)
    CMP16 = view(HTR, 7840, 1024, F32)
    TMP16 = view(HTR, 8864, 1024, F32)
    T8B = view(HTR, 9888, 512, F32)
    T4B = view(HTR, 10400, 256, F32)
    T2B = view(HTR, 10656, 128, F32)
    RKE = view(HTR, 10784, 1024, F32)
    SELB = view(HTR, 11808, 1024, F32)
    gwf_sb = HTR[0:16, 12832:21024].bitcast(F32)
    gwb_sb = HTR[0:16, 4096:8192].bitcast(BF16)
    ropeCt = view(HTR, 0, 2048, BF16, a=TC)
    ropeSAt = view(HTR, 2048, 2048, BF16, a=TC)
    ropeSBt = view(HTR, 4096, 2048, BF16, a=TC)
    HB1 = view(HTR, 6144, 8192, BF16, a=TC)
    HB2 = view(HTR, 14336, 8192, BF16, a=TC)

    DUMP = view(GNR, 0, 8192, F32)
    scrbf = view(GNR, 8192, 4096, BF16)
    TT8b = view(GNR, 0, 4096, BF16, a=64)
    TT8f = GNR[:, 0:4096].bitcast(F32)
    TT4 = view(GNR, 4096, 4096, F32, a=64)
    qsB = view(GNR, 8192, 4096, F32, a=4)
    PS = view(GNR, 10240, 512, F32)
    PSQ = view(GNR, 10752, 512, F32)
    MB = view(GNR, 11264, 512, F32)
    VARB2 = view(GNR, 11776, 512, F32)
    RSB2 = view(GNR, 12288, 512, F32)
    RTB = view(GNR, 12800, 512, F32)

    P = [es.enter_context(nc.psum_tensor(f"P{k}", [128, 512], F32)) for k in range(8)]
    Pbf = [P[k][:, 0:64].bitcast(BF16) for k in range(8)]

    S = Sched(nc, es)
    for ph in ("P", "A", "B", "C"):
        for kind in ("q", "t", "v", "s"):
            S.sem(kind + ph)

    q, t, v, s = "qP", "tP", "vP", "sP"
    x_ready = S.ev("g", lambda g: g.dma_start(
        xtok[:], dr["X"].rearrange("(c p) d -> p c d", p=128)), inc=q, n=16)
    S.ev("g", lambda g: g.dma_start(ebt[:], dr["EB"]), inc=q, n=16)
    S.ev("g", lambda g: g.dma_start(hhbt[:], dr["HHB"]), inc=q, n=16)
    S.ev("g", lambda g: g.dma_start(gnwt[:], dr["GNW"]), inc=q, n=16)
    S.ev("g", lambda g: g.dma_start(gnbt[:], dr["GNB"]), inc=q, n=16)
    S.ev("g", lambda g: g.dma_start(identt[:], d_ident), inc=q, n=16)
    S.ev("g", lambda g: g.dma_start(low16t[:], d_low16), inc=q, n=16)
    S.ev("g", lambda g: g.dma_start(low4t[:], d_low4), inc=q, n=16)
    gw_dma = S.ev("g", lambda g: g.dma_start(gwf_sb, dr["GW"]), inc=q, n=16)
    gwb_done = S.ev("v", lambda vv: vv.tensor_copy(gwb_sb, gwf_sb), waits=[gw_dma], inc=v)
    pb_free = [None, None]
    for c in range(KC):
        bk = 6 + (c % 2)
        tm = S.ev("t", (lambda c=c, bk=bk: (lambda tt: tt.transpose(
            Pbf[bk][:, 0:16], gwb_sb[:, c * 128:(c + 1) * 128], identt[0:16, 0:16])))(),
            waits=[gwb_done, pb_free[c % 2]], inc=t)
        pb_free[c % 2] = S.ev("s", (lambda c=c, bk=bk: (lambda ss: ss.copy(
            gwT[:, c, :], Pbf[bk][:, 0:16])))(), waits=[tm], inc=s)
    PH = {"P": {k: S.mark(k + "P") for k in "qtvs"}}

    def mm_group(psum_ap, st_fn, mv_fn, nk):
        def emit(tt):
            mm = None
            for k in range(nk):
                mm = tt.matmul(psum_ap, st_fn(k), mv_fn(k), start=(k == 0), stop=(k == nk - 1))
            return mm
        return emit

    def emit_ln(ph, w0):
        q, t, v, s = "q" + ph, "t" + ph, "v" + ph, "s" + ph
        pe = None
        for tau in range(TC):
            pe = S.ev("s", (lambda tau=tau: (lambda ss: ss.activation(
                DUMP, xtok[:, tau, :], AF.Copy, accum_out=ssum[:, tau:tau + 1])))(),
                waits=(list(w0) if tau == 0 else []) + [pe], inc=s)
            pe = S.ev("s", (lambda tau=tau: (lambda ss: ss.activation(
                DUMP, xtok[:, tau, :], AF.Square, accum_out=ssq[:, tau:tau + 1])))(),
                waits=[pe], inc=s)
        stats = S.mark(s)
        e = S.ev("v", lambda vv: vv.tensor_scalar(meanb[:], ssum[:], 1.0 / D, None, ALU.mult),
                 waits=[stats], inc=v)
        e = S.ev("v", lambda vv: vv.tensor_tensor(msqb[:], meanb[:], meanb[:], ALU.mult),
                 waits=[e], inc=v)
        e2 = S.ev("v", lambda vv: vv.tensor_scalar(varb[:], ssq[:], 1.0 / D, 1e-5,
                                                   ALU.mult, ALU.add), waits=[stats], inc=v)
        e = S.ev("v", lambda vv: vv.tensor_tensor(varb[:], varb[:], msqb[:], ALU.subtract),
                 waits=[e, e2], inc=v)
        e = S.ev("s", lambda ss: ss.activation(rsb[:], varb[:], AF.Ln),
                 waits=[e], inc=s)
        rs_done = S.ev("s", lambda ss: ss.activation(rsb[:], rsb[:], AF.Exp, scale=-0.5),
                       waits=[e], inc=s)
        tbank_free = [None, None]
        trans_done = None
        for tau in range(TC):
            vm = S.ev("v", (lambda tau=tau: (lambda vv: vv.tensor_scalar(
                scrbf, xtok[:, tau, :], meanb[:, tau:tau + 1], rsb[:, tau:tau + 1],
                ALU.subtract, ALU.mult)))(), waits=[rs_done, trans_done], inc=v)
            for c in range(KC):
                bk = 6 + (c % 2)
                tm = S.ev("t", (lambda c=c, bk=bk: (lambda tt: tt.transpose(
                    Pbf[bk], scrbf[:, c * 128:(c + 1) * 128], identt[:])))(),
                    waits=[vm, tbank_free[c % 2]], inc=t)
                tbank_free[c % 2] = S.ev("s", (lambda c=c, tau=tau, bk=bk: (lambda ss: ss.copy(
                    lnxT[:, c, tau * 128:(tau + 1) * 128], Pbf[bk])))(), waits=[tm], inc=s)
            trans_done = S.mark(t)
        return S.mark(s), tbank_free

    def ffn_phase(ph, prev):
        q, t, v, s = "q" + ph, "t" + ph, "v" + ph, "s" + ph
        w0 = [prev["q"], prev["t"], prev["v"], prev["s"], x_ready]
        ln_done, tbank_free = emit_ln(ph, w0)

        sbank_free = [None, None]
        for tau in range(TC):
            bk = tau % 2
            tm = S.ev("t", (lambda tau=tau, bk=bk: mm_group(
                P[bk][:, 0:16],
                lambda k, tau=tau: lnxT[:, k, tau * 128:(tau + 1) * 128],
                lambda k: gwT[:, k, :], KC))(),
                waits=[ln_done, sbank_free[bk]], inc=t)
            sbank_free[bk] = S.ev("s", (lambda tau=tau, bk=bk: (lambda ss: ss.activation(
                ETB[:, tau * 16:(tau + 1) * 16], P[bk][:, 0:16], AF.Exp, scale=-1.0)))(),
                waits=[tm], inc=s)
        em = S.mark(s)
        e = S.ev("v", lambda vv: vv.tensor_scalar(D1B, ETB, 1.0, None, ALU.add),
                 waits=[em], inc=v)
        e = S.ev("v", lambda vv: vv.reciprocal(SCO, D1B), waits=[e], inc=v)
        e = S.ev("v", lambda vv: vv.tensor_tensor(
            BIA.rearrange("p (a b) -> p a b", a=TC), SCO.rearrange("p (a b) -> p a b", a=TC),
            ebt[:].unsqueeze(1).broadcast_to([128, TC, NE]), ALU.add), waits=[e], inc=v)
        for tau in range(TC):
            for gg in range(4):
                e = S.ev("v", (lambda tau=tau, gg=gg: (lambda vv: vv.reduce_max(
                    GSB[:, tau * 4 + gg:tau * 4 + gg + 1],
                    BIA[:, tau * 16 + gg * 4:tau * 16 + gg * 4 + 4], axis=AX.X)))(),
                    waits=[e], inc=v)
        for tau in range(TC):
            a4 = GSB[:, tau * 4:tau * 4 + 4].unsqueeze(2).broadcast_to([128, 4, 4])
            b4 = GSB[:, tau * 4:tau * 4 + 4].unsqueeze(1).broadcast_to([128, 4, 4])
            c3 = CMP4.rearrange("p (a b) -> p a b", a=4)
            t3v = TMP4.rearrange("p (a b) -> p a b", a=4)
            e = S.ev("v", (lambda a4=a4, b4=b4, c3=c3: (lambda vv: vv.tensor_tensor(
                c3, a4, b4, ALU.is_gt)))(), waits=[e], inc=v)
            e = S.ev("v", (lambda a4=a4, b4=b4, t3v=t3v: (lambda vv: vv.tensor_tensor(
                t3v, a4, b4, ALU.is_equal)))(), inc=v)
            e = S.ev("v", (lambda t3v=t3v: (lambda vv: vv.tensor_tensor(
                t3v, t3v, low4t[:].rearrange("p (a b) -> p a b", a=4), ALU.mult)))(),
                waits=[e], inc=v)
            e = S.ev("v", (lambda c3=c3, t3v=t3v: (lambda vv: vv.tensor_tensor(
                c3, c3, t3v, ALU.add)))(), waits=[e], inc=v)
            e = S.ev("v", (lambda c3=c3: (lambda vv: vv.tensor_tensor(
                U8B.rearrange("p (a b) -> p a b", a=2), c3[:, 0:2, :], c3[:, 2:4, :],
                ALU.add)))(), waits=[e], inc=v)
            e = S.ev("v", (lambda tau=tau: (lambda vv: vv.tensor_tensor(
                RKG[:, tau * 4:tau * 4 + 4], U8B[:, 0:4], U8B[:, 4:8], ALU.add)))(),
                waits=[e], inc=v)
        e = S.ev("v", lambda vv: vv.tensor_single_scalar(RKG, RKG, 2.0, ALU.is_lt),
                 waits=[e], inc=v)
        for tau in range(TC):
            e = S.ev("v", (lambda tau=tau: (lambda vv: vv.tensor_copy(
                GMR[:, tau * 16:(tau + 1) * 16].rearrange("p (a b) -> p a b", a=4),
                RKG[:, tau * 4:tau * 4 + 4].unsqueeze(2).broadcast_to([128, 4, 4]))))(),
                waits=[e], inc=v)
        gmr_done = e
        e = S.ev("v", lambda vv: vv.tensor_scalar(MZB, GMR, 1e30, -1e30, ALU.mult, ALU.add),
                 waits=[gmr_done], inc=v)
        e = S.ev("v", lambda vv: vv.tensor_tensor(MSKB, BIA, GMR, ALU.mult),
                 waits=[e], inc=v)
        e = S.ev("v", lambda vv: vv.tensor_tensor(MSKB, MSKB, MZB, ALU.add),
                 waits=[e], inc=v)
        for tau in range(TC):
            a16 = MSKB[:, tau * 16:(tau + 1) * 16].unsqueeze(2).broadcast_to([128, 16, 16])
            b16 = MSKB[:, tau * 16:(tau + 1) * 16].unsqueeze(1).broadcast_to([128, 16, 16])
            c3 = CMP16.rearrange("p (a b) -> p a b", a=16)
            t3v = TMP16.rearrange("p (a b) -> p a b", a=16)
            e = S.ev("v", (lambda a16=a16, b16=b16, c3=c3: (lambda vv: vv.tensor_tensor(
                c3, a16, b16, ALU.is_gt)))(), waits=[e], inc=v)
            e = S.ev("v", (lambda a16=a16, b16=b16, t3v=t3v: (lambda vv: vv.tensor_tensor(
                t3v, a16, b16, ALU.is_equal)))(), inc=v)
            e = S.ev("v", (lambda t3v=t3v: (lambda vv: vv.tensor_tensor(
                t3v, t3v, low16t[:].rearrange("p (a b) -> p a b", a=16), ALU.mult)))(),
                waits=[e], inc=v)
            e = S.ev("v", (lambda c3=c3, t3v=t3v: (lambda vv: vv.tensor_tensor(
                c3, c3, t3v, ALU.add)))(), waits=[e], inc=v)
            e = S.ev("v", (lambda c3=c3: (lambda vv: vv.tensor_tensor(
                T8B.rearrange("p (a b) -> p a b", a=8), c3[:, 0:8, :], c3[:, 8:16, :],
                ALU.add)))(), waits=[e], inc=v)
            e = S.ev("v", (lambda vv: vv.tensor_tensor(
                T4B.rearrange("p (a b) -> p a b", a=4),
                T8B.rearrange("p (a b) -> p a b", a=8)[:, 0:4, :],
                T8B.rearrange("p (a b) -> p a b", a=8)[:, 4:8, :], ALU.add)),
                waits=[e], inc=v)
            e = S.ev("v", (lambda vv: vv.tensor_tensor(
                T2B.rearrange("p (a b) -> p a b", a=2),
                T4B.rearrange("p (a b) -> p a b", a=4)[:, 0:2, :],
                T4B.rearrange("p (a b) -> p a b", a=4)[:, 2:4, :], ALU.add)),
                waits=[e], inc=v)
            e = S.ev("v", (lambda tau=tau: (lambda vv: vv.tensor_tensor(
                RKE[:, tau * 16:(tau + 1) * 16], T2B[:, 0:16], T2B[:, 16:32], ALU.add)))(),
                waits=[e], inc=v)
        e = S.ev("v", lambda vv: vv.tensor_single_scalar(SELB, RKE, 4.0, ALU.is_lt),
                 waits=[e], inc=v)
        e = S.ev("v", lambda vv: vv.tensor_tensor(SELB, SELB, SCO, ALU.mult),
                 waits=[e], inc=v)
        for tau in range(TC):
            e = S.ev("v", (lambda tau=tau: (lambda vv: vv.reduce_sum(
                denb[:, tau:tau + 1], SELB[:, tau * 16:(tau + 1) * 16], axis=AX.X)))(),
                waits=[e], inc=v)
        e = S.ev("v", lambda vv: vv.reciprocal(recb[:], denb[:]), waits=[e], inc=v)
        cw_done = S.ev("v", lambda vv: vv.scalar_tensor_tensor(
            cw[:].rearrange("p (a b) -> p a b", a=TC),
            SELB.rearrange("p (a b) -> p a b", a=TC), 0.5,
            recb[:].unsqueeze(2).broadcast_to([128, TC, NE]), ALU.mult, ALU.mult),
            waits=[e], inc=v)

        if ph == "A" and "cw" in dbg:
            S.ev("g", lambda g: g.dma_start(dbg["cw"], cw[:]), waits=[cw_done], inc=q, n=16)
        if ph == "A" and "scores" in dbg:
            S.ev("g", lambda g: g.dma_start(dbg["scores"], SCO), waits=[cw_done], inc=q, n=16)
        if ph == "A" and "lnxT" in dbg:
            S.ev("g", lambda g: g.dma_start(
                dbg["lnxT"].rearrange("(p c) t -> p c t", p=128), lnxT[:]),
                waits=[cw_done, ln_done], inc=q, n=16)

        hbank_free = [sbank_free[0], sbank_free[1], None, None]
        ybank_free = [None, None, tbank_free[0], tbank_free[1]]
        st_free = {("1", 0): None, ("1", 1): None, ("3", 0): None, ("3", 1): None,
                   ("2", 0): None, ("2", 1): None}
        silub_free = [None, None]
        first_dma = True

        sources = ([("e", i) for i in range(cfg["n_experts"])]
                   + [("s", p) for p in range(cfg["n_shared"])])
        for (kind, idx) in sources:
            for th in range(2):
                w1m, w3m, w2m = {}, {}, {}

                def emit_w13(i, extra=()):
                    if kind == "e":
                        src1 = dr["W1"][idx, :, i * 128:(i + 1) * 128]
                        src3 = dr["W3"][idx, :, i * 128:(i + 1) * 128]
                    else:
                        col = idx * INTER + i * 128
                        src1 = dr["SW1"][:, col:col + 128]
                        src3 = dr["SW3"][:, col:col + 128]
                    r = i % 2
                    w1m[i] = S.ev("g", (lambda src1=src1, r=r: (lambda g: g.dma_start(
                        st1v(r), src1.rearrange("(c p) i -> p c i", p=128))))(),
                        waits=[st_free[("1", r)]] + list(extra), inc=q, n=16)
                    w3m[i] = S.ev("g", (lambda src3=src3, r=r: (lambda g: g.dma_start(
                        st3v(r), src3.rearrange("(c p) i -> p c i", p=128))))(),
                        waits=[st_free[("3", r)]], inc=q, n=16)

                def emit_w2(dc):
                    if kind == "e":
                        src2 = dr["W2"][idx, :, dc * 256:(dc + 1) * 256]
                    else:
                        src2 = dr["SW2"][idx * INTER:(idx + 1) * INTER, dc * 256:(dc + 1) * 256]
                    r = dc % 2
                    w2m[dc] = S.ev("g", (lambda src2=src2, r=r: (lambda g: g.dma_start(
                        stw2[:, r, :, :], src2.rearrange("(c p) d -> p c d", p=128))))(),
                        waits=[st_free[("2", r)]], inc=q, n=16)

                emit_w13(0, extra=([prev["t"]] if first_dma else []))
                first_dma = False
                emit_w13(1)
                emit_w2(0)
                emit_w2(1)

                for i in range(IC):
                    r = i % 2
                    for ct in range(2):
                        bA, bB = ct, 2 + ct
                        toff = th * 1024 + ct * 512
                        mA = S.ev("t", (lambda r=r, toff=toff, bA=bA: mm_group(
                            P[bA][:, 0:512],
                            lambda k, r=r: st1v(r)[:, k, :],
                            lambda k, toff=toff: lnxT[:, k, toff:toff + 512], KC))(),
                            waits=[w1m[i], ln_done, hbank_free[bA]], inc=t)
                        mB = S.ev("t", (lambda r=r, toff=toff, bB=bB: mm_group(
                            P[bB][:, 0:512],
                            lambda k, r=r: st3v(r)[:, k, :],
                            lambda k, toff=toff: lnxT[:, k, toff:toff + 512], KC))(),
                            waits=[w3m[i], ln_done, hbank_free[bB]], inc=t)
                        sl = ct
                        sA = S.ev("s", (lambda bA=bA, sl=sl: (lambda ss: ss.activation(
                            silub[:, sl, :], P[bA][:, 0:512], AF.Silu)))(),
                            waits=[mA, silub_free[sl]], inc=s)
                        hbank_free[bA] = sA
                        hv_ = S.ev("v", (lambda i=i, ct=ct, sl=sl, bB=bB: (lambda vv: vv.tensor_tensor(
                            hT[:, i, ct * 512:(ct + 1) * 512], silub[:, sl, :], P[bB][:, 0:512],
                            ALU.mult)))(), waits=[sA, mB], inc=v)
                        hbank_free[bB] = hv_
                        silub_free[sl] = hv_
                        if ct == 1:
                            st_free[("1", r)] = mA
                            st_free[("3", r)] = mB
                    if i + 2 < IC:
                        emit_w13(i + 2)
                h_done = S.mark(v)

                for dc in range(8):
                    r = dc % 2
                    last_y = None
                    for tj in range(8):
                        tau = th * 8 + tj
                        bk = 4 + (tj % 4)
                        ym = S.ev("t", (lambda r=r, tj=tj, bk=bk: mm_group(
                            P[bk][:, 0:256],
                            lambda i, tj=tj: hT[:, i, tj * 128:tj * 128 + 128],
                            lambda i, r=r: stw2[:, r, i, :], IC))(),
                            waits=[w2m[dc], h_done, ybank_free[tj % 4]], inc=t)
                        last_y = ym
                        sc = (cw[:, tau * 16 + idx:tau * 16 + idx + 1] if kind == "e" else 0.5)
                        ybank_free[tj % 4] = S.ev("v", (lambda tau=tau, dc=dc, bk=bk, sc=sc:
                            (lambda vv: vv.scalar_tensor_tensor(
                                xtok[:, tau, dc * 256:(dc + 1) * 256], P[bk][:, 0:256], sc,
                                xtok[:, tau, dc * 256:(dc + 1) * 256], ALU.mult, ALU.add)))(),
                            waits=[ym, cw_done], inc=v)
                    st_free[("2", r)] = last_y
                    if dc + 2 < 8:
                        emit_w2(dc + 2)

        if ph in ("A",) and "hT" in dbg:
            m = {k: S.mark(k + ph) for k in "qtvs"}
            S.ev("g", lambda g: g.dma_start(
                dbg["hT"].rearrange("(p c) t -> p c t", p=128), hT), waits=[m["v"]], inc=q, n=16)
        if ph in ("A",) and "xph" in dbg:
            m = {k: S.mark(k + ph) for k in "qtvs"}
            S.ev("g", lambda g: g.dma_start(
                dbg["xph"].rearrange("(c p) d -> p c d", p=128), xtok[:]),
                waits=[m["v"], m["t"]], inc=q, n=16)
        return {k: S.mark(k + ph) for k in "qtvs"}

    def st1v(r):
        return STU[:, 0:8192].bitcast(BF16).rearrange("p (r c i) -> p r c i", r=2, c=KC)[:, r, :, :]

    def st3v(r):
        return STU[:, 8192:16384].bitcast(BF16).rearrange("p (r c i) -> p r c i", r=2, c=KC)[:, r, :, :]

    def wqv(r):
        return STU[:, 0:16384].bitcast(BF16).rearrange("p (r c i) -> p r c i", r=2, c=KC)[:, r, :, :]

    def attn_phase(prev):
        ph = "B"
        q, t, v, s = "qB", "tB", "vB", "sB"
        w0 = [prev["q"], prev["t"], prev["v"], prev["s"]]
        rt0 = S.ev("g", lambda g: g.dma_start(ropeCt, d_ropeC), waits=w0, inc=q, n=16)
        S.ev("g", lambda g: g.dma_start(ropeSAt, d_ropeSA), inc=q, n=16)
        S.ev("g", lambda g: g.dma_start(ropeSBt, d_ropeSB), inc=q, n=16)
        tab_done = S.mark(q)

        ln_done, tbank_free = emit_ln(ph, w0)

        wq_free = [None, None]
        qbank_free = [None] * 6
        qs_free = [None, None, None, None]
        tp_free = [tbank_free[0], tbank_free[1]]
        h_war = None
        for cb in range(8):
            r = cb % 2
            wqm = S.ev("g", (lambda cb=cb, r=r: (lambda g: g.dma_start(
                wqv(r), dr["WQ"][:, cb * 256:(cb + 1) * 256]
                .rearrange("(c p) d -> p c d", p=128))))(),
                waits=[wq_free[r], tab_done if cb == 0 else None,
                       prev["t"] if cb == 0 else None], inc=q, n=16)
            tanh_marks = []
            pend = []
            def emit_tanh(tau, sr, ev):
                tm = S.ev("s", (lambda sr=sr, tau=tau: (lambda ss: ss.activation(
                    HB1[:, tau, :], qsB[:, sr, :], AF.Tanh)))(),
                    waits=[ev, h_war], inc=s)
                qs_free[sr] = tm
                tanh_marks.append(tm)
            for tau in range(TC):
                bk = tau % 6
                qm = S.ev("t", (lambda tau=tau, r=r, bk=bk: mm_group(
                    P[bk][:, 0:256],
                    lambda k, tau=tau: lnxT[:, k, tau * 128:(tau + 1) * 128],
                    lambda k, r=r: wqv(r)[:, k, :], KC))(),
                    waits=[wqm, ln_done, qbank_free[bk]], inc=t)
                sr = tau % 4
                if cb == 0:
                    cm = S.ev("s", (lambda bk=bk, sr=sr: (lambda ss: ss.copy(
                        qsB[:, sr, :], P[bk][:, 0:256])))(),
                        waits=[qm, qs_free[sr]], inc=s)
                else:
                    cm = S.ev("s", (lambda bk=bk, sr=sr: (lambda ss: ss.activation(
                        qsB[:, sr, :], P[bk][:, 0:256], AF.Relu)))(),
                        waits=[qm, qs_free[sr]], inc=s)
                qbank_free[bk] = cm
                ev = cm
                if cb == 0:
                    ev = S.ev("v", (lambda sr=sr, tau=tau: (lambda vv: vv.tensor_tensor(
                        RTB[:, 0:64], qsB[:, sr, 0:64], ropeCt[:, tau, :], ALU.mult)))(),
                        waits=[ev, rt0], inc=v)
                    ev = S.ev("v", (lambda sr=sr, tau=tau: (lambda vv: vv.tensor_tensor(
                        RTB[:, 64:128], qsB[:, sr, 1:65], ropeSAt[:, tau, :], ALU.mult)))(),
                        inc=v)
                    ev = S.ev("v", (lambda sr=sr, tau=tau: (lambda vv: vv.tensor_tensor(
                        TT8f[:, 0:63], qsB[:, sr, 0:63], ropeSBt[:, tau, 0:63], ALU.mult)))(),
                        inc=v)
                    ev = S.ev("v", (lambda sr=sr: (lambda vv: vv.tensor_tensor(
                        qsB[:, sr, 0:64], RTB[:, 0:64], RTB[:, 64:128], ALU.add)))(),
                        waits=[ev], inc=v)
                    ev = S.ev("v", (lambda sr=sr: (lambda vv: vv.tensor_tensor(
                        qsB[:, sr, 1:64], qsB[:, sr, 1:64], TT8f[:, 0:63], ALU.add)))(),
                        waits=[ev], inc=v)
                    ev = S.ev("v", (lambda sr=sr: (lambda vv: vv.tensor_scalar_max(
                        qsB[:, sr, :], qsB[:, sr, :], 0.0)))(), waits=[ev], inc=v)
                ev = S.ev("v", (lambda sr=sr, cb=cb: (lambda vv: vv.tensor_tensor(
                    qsB[:, sr, :], qsB[:, sr, :], hhbt[:, cb * 256:(cb + 1) * 256],
                    ALU.add)))(), waits=[ev], inc=v)
                pend.append((tau, sr, ev))
                if len(pend) >= 3:
                    emit_tanh(*pend.pop(0))
            while pend:
                emit_tanh(*pend.pop(0))
            hdone = tanh_marks[-1]
            H3 = HB1.rearrange("p a b -> p (a b)").rearrange("p (a b) -> p a b", a=64)
            SQ3 = HB2.rearrange("p a b -> p (a b)").rearrange("p (a b) -> p a b", a=64)
            e = S.ev("v", lambda vv, H3=H3, SQ3=SQ3: vv.tensor_tensor(SQ3, H3, H3, ALU.mult),
                     waits=[hdone], inc=v)

            def tree(src3, dst_ps, e):
                L1 = TT8b
                e = S.ev("v", (lambda src3=src3: (lambda vv: vv.tensor_tensor(
                    L1, src3[:, :, 0:32], src3[:, :, 32:64], ALU.add)))(), waits=[e], inc="vB")
                e = S.ev("v", lambda vv: vv.tensor_tensor(
                    TT4, TT8b[:, :, 0:16], TT8b[:, :, 16:32], ALU.add), waits=[e], inc="vB")
                L3 = TT8f[:, 0:512].rearrange("p (a b) -> p a b", a=64)
                e = S.ev("v", lambda vv, L3=L3: vv.tensor_tensor(
                    L3, TT4[:, :, 0:8], TT4[:, :, 8:16], ALU.add), waits=[e], inc="vB")
                L4 = TT4.rearrange("p a b -> p (a b)")[:, 0:256].rearrange("p (a b) -> p a b", a=64)
                e = S.ev("v", lambda vv, L3=L3, L4=L4: vv.tensor_tensor(
                    L4, L3[:, :, 0:4], L3[:, :, 4:8], ALU.add), waits=[e], inc="vB")
                L5 = TT8f[:, 512:640].rearrange("p (a b) -> p a b", a=64)
                e = S.ev("v", lambda vv, L4=L4, L5=L5: vv.tensor_tensor(
                    L5, L4[:, :, 0:2], L4[:, :, 2:4], ALU.add), waits=[e], inc="vB")
                L5f = TT8f[:, 512:640]
                e = S.ev("v", lambda vv, L5f=L5f, dst_ps=dst_ps: vv.tensor_tensor(
                    dst_ps[:, 0:127], L5f[:, 0:127], L5f[:, 1:128], ALU.add),
                    waits=[e], inc="vB")
                return e

            e = tree(H3, PS, e)
            e = tree(SQ3, PSQ, e)
            e = S.ev("v", lambda vv: vv.tensor_scalar(MB, PS, 1.0 / 64, None, ALU.mult),
                     waits=[e], inc=v)
            e = S.ev("v", lambda vv: vv.tensor_tensor(VARB2, MB, MB, ALU.mult),
                     waits=[e], inc=v)
            e = S.ev("v", lambda vv: vv.scalar_tensor_tensor(
                VARB2, PSQ, 1.0 / 64, VARB2, ALU.mult, ALU.subtract), waits=[e], inc=v)
            e = S.ev("v", lambda vv: vv.tensor_scalar(VARB2, VARB2, 1.0, 1e-5,
                                                      ALU.mult, ALU.add), waits=[e], inc=v)
            e = S.ev("s", lambda ss: ss.activation(RSB2, VARB2, AF.Ln),
                     waits=[e], inc=s)
            e = S.ev("s", lambda ss: ss.activation(RSB2, RSB2, AF.Exp, scale=-0.5),
                     waits=[e], inc=s)
            m_rep = MB.rearrange("p (g w) -> p g w", g=64)[:, :, 0:1].broadcast_to([128, 64, 64])
            rs_rep = RSB2.rearrange("p (g w) -> p g w", g=64)[:, :, 0:1].broadcast_to([128, 64, 64])
            e = S.ev("v", lambda vv, m_rep=m_rep, H3=H3, SQ3=SQ3: vv.scalar_tensor_tensor(
                SQ3, m_rep, -1.0, H3, ALU.mult, ALU.add), waits=[e], inc=v)
            e = S.ev("v", lambda vv, rs_rep=rs_rep, SQ3=SQ3: vv.tensor_tensor(
                SQ3, SQ3, rs_rep, ALU.mult), waits=[e], inc=v)
            gw_rep = (lambda cb=cb: gnwt[:, cb * 256:(cb + 1) * 256]
                      .unsqueeze(1).broadcast_to([128, TC, 256]))()
            gb_rep = (lambda cb=cb: gnbt[:, cb * 256:(cb + 1) * 256]
                      .unsqueeze(1).broadcast_to([128, TC, 256]))()
            e = S.ev("v", lambda vv, gw_rep=gw_rep: vv.tensor_tensor(
                HB1, HB2, gw_rep, ALU.mult), waits=[e], inc=v)
            gdone = S.ev("v", lambda vv, gb_rep=gb_rep: vv.tensor_tensor(
                HB1, HB1, gb_rep, ALU.add), waits=[e], inc=v)
            last_t = None
            for tau in range(TC):
                for j in range(2):
                    bk = 6 + j
                    tm = S.ev("t", (lambda tau=tau, j=j, bk=bk: (lambda tt: tt.transpose(
                        Pbf[bk], HB1[:, tau, j * 128:(j + 1) * 128], identt[:])))(),
                        waits=[gdone, tp_free[j]], inc=t)
                    last_t = tm
                    tp_free[j] = S.ev("v", (lambda tau=tau, j=j, cb=cb, bk=bk:
                        (lambda vv: vv.tensor_tensor(
                            xtok[:, 2 * cb + j, tau * 128:(tau + 1) * 128],
                            xtok[:, 2 * cb + j, tau * 128:(tau + 1) * 128],
                            Pbf[bk], ALU.add)))(), waits=[tm], inc=v)
            h_war = last_t
            wq_free[r] = S.mark(t)
        return {k: S.mark(k + "B") for k in "qtvs"}

    prev = PH["P"]
    if "A" in cfg["phases"]:
        prev = ffn_phase("A", prev)
        PH["A"] = prev
    if "B" in cfg["phases"]:
        prev = attn_phase(prev)
        PH["B"] = prev
    if "C" in cfg["phases"]:
        prev = ffn_phase("C", prev)
        PH["C"] = prev

    qf = "q" + cfg["phases"][-1]
    out_dma = S.ev("g", lambda g: g.dma_start(
        dr["OUT"].rearrange("(c p) d -> p c d", p=128), xtok[:]),
        waits=[prev["v"], prev["t"], prev["s"]], inc=qf, n=16)

    with nc.Block() as blk:
        @blk.gpsimd
        def _(g):
            S.run(g, "g")

        @blk.tensor
        def _(tt):
            S.run(tt, "t")

        @blk.vector
        def _(vv):
            S.run(vv, "v")

        @blk.scalar
        def _(ss):
            S.run(ss, "s")

        @blk.sync
        def _(sy):
            sy.wait_ge(S.sems[out_dma[0]], out_dma[1])

    return nc, es


def _in_maps(inputs):
    X = np.ascontiguousarray(np.asarray(inputs["L0_attn_X"], dtype=np.float32))
    wq = np.ascontiguousarray(np.asarray(inputs["L0_attn_W_Q"], dtype=np.float32))
    gw = np.ascontiguousarray(np.asarray(inputs["L0_ffn_gate_w"], dtype=np.float32))
    w1 = np.ascontiguousarray(np.asarray(inputs["L0_ffn_w1"], dtype=np.float32))
    w2 = np.ascontiguousarray(np.asarray(inputs["L0_ffn_w2"], dtype=np.float32))
    w3 = np.ascontiguousarray(np.asarray(inputs["L0_ffn_w3"], dtype=np.float32))
    sw1 = np.ascontiguousarray(np.asarray(inputs["L0_ffn_sw1"], dtype=np.float32))
    sw2 = np.ascontiguousarray(np.asarray(inputs["L0_ffn_sw2"], dtype=np.float32))
    sw3 = np.ascontiguousarray(np.asarray(inputs["L0_ffn_sw3"], dtype=np.float32))
    eb = np.broadcast_to(np.asarray(inputs["L0_ffn_expert_bias"], np.float32)
                         .reshape(1, NE), (128, NE)).copy()
    hhb = np.broadcast_to(np.asarray(inputs["L0_attn_hh_b"], np.float32)
                          .reshape(1, D), (128, D)).copy()
    gnw = np.broadcast_to(np.asarray(inputs["L0_attn_gn_weight"], np.float32)
                          .reshape(1, D), (128, D)).copy()
    gnb = np.broadcast_to(np.asarray(inputs["L0_attn_gn_bias"], np.float32)
                          .reshape(1, D), (128, D)).copy()
    maps = []
    for b in range(NCORES):
        maps.append({"X": np.ascontiguousarray(X[b]), "WQ": wq, "GW": gw,
                     "W1": w1, "W2": w2, "W3": w3,
                     "SW1": sw1, "SW2": sw2, "SW3": sw3,
                     "EB": eb, "HHB": hhb, "GNW": gnw, "GNB": gnb})
    return maps


def kernel(**inputs):
    maps = _in_maps(inputs)
    nc, es = _build()
    res = run_bass_kernel_spmd(nc, maps, core_ids=list(range(NCORES)))
    out = np.stack([np.asarray(res.results[b]["OUT"]) for b in range(NCORES)], axis=0)
    es.close()
    return out


# version 03
# speedup vs baseline: 1.0121x; 1.0121x over previous
import numpy as np
from contextlib import ExitStack

import concourse.bass as bass
import concourse.mybir as mybir
from concourse.bass_utils import run_bass_kernel_spmd

F32 = mybir.dt.float32
BF16 = mybir.dt.bfloat16
U8 = mybir.dt.uint8
AF = mybir.ActivationFunctionType
ALU = mybir.AluOpType
AX = mybir.AxisListType

NCORES = 8
T = 2048
D = 2048
TC = 16
KC = 16
INTER = 1408
IC = 11
NE = 16
ROPE = 64

CFG = {
    "n_experts": NE,
    "n_shared": 2,
    "phases": ("A", "B", "C"),
    "debug": (),
}


def _rope_tables_np():
    inv = 1.0 / 10000.0 ** (np.arange(0, ROPE, 2) / ROPE)
    fr = np.repeat(np.arange(T)[:, None] * inv[None, :], 2, axis=1)
    C = np.cos(fr).astype(np.float32)
    S = np.sin(fr).astype(np.float32)
    j = np.arange(ROPE)
    SA = np.where(j[None, :] % 2 == 0, -S, 0.0).astype(np.float32)
    SB = np.zeros((T, ROPE), np.float32)
    SB[:, 0:63] = np.where(j[None, 0:63] % 2 == 0, S[:, 1:64], 0.0)
    lay = lambda a: np.ascontiguousarray(a.reshape(TC, 128, ROPE).transpose(1, 0, 2))
    return lay(C), lay(SA), lay(SB)


class Sched:
    def __init__(self, nc, es):
        self.nc, self.es = nc, es
        self.events = []
        self.sems = {}
        self.cnt = {}

    def sem(self, name):
        self.sems[name] = self.es.enter_context(self.nc.semaphore(name))
        self.cnt[name] = 0

    def ev(self, engine, emit, waits=(), inc=None, n=1):
        w = tuple((s, v) for m in waits if m is not None for (s, v) in [m] if v > 0)
        self.events.append((engine, w, emit, inc, n))
        if inc is not None:
            self.cnt[inc] += n
            return (inc, self.cnt[inc])
        return None

    def mark(self, sem):
        return (sem, self.cnt[sem])

    def run(self, eng, name):
        lastw = {}
        for (en, waits, emit, inc, n) in self.events:
            if en != name:
                continue
            for (s, v) in waits:
                if lastw.get(s, -1) >= v:
                    continue
                eng.wait_ge(self.sems[s], v)
                lastw[s] = v
            inst = emit(eng)
            if inc is not None:
                assert inst is not None, "inc event emitted no instruction"
                inst.then_inc(self.sems[inc], n)


def _build(cfg=CFG):
    import ml_dtypes
    nc = bass.Bass("TRN2", target_bir_lowering=False, debug=False)
    es = ExitStack()

    dr = {}
    dr["X"] = nc.dram_tensor("X", [T, D], F32, kind="ExternalInput").ap()
    dr["WQ"] = nc.dram_tensor("WQ", [D, D], F32, kind="ExternalInput").ap()
    dr["GW"] = nc.dram_tensor("GW", [NE, D], F32, kind="ExternalInput").ap()
    dr["W1"] = nc.dram_tensor("W1", [NE, D, INTER], F32, kind="ExternalInput").ap()
    dr["W2"] = nc.dram_tensor("W2", [NE, INTER, D], F32, kind="ExternalInput").ap()
    dr["W3"] = nc.dram_tensor("W3", [NE, D, INTER], F32, kind="ExternalInput").ap()
    dr["SW1"] = nc.dram_tensor("SW1", [D, 2 * INTER], F32, kind="ExternalInput").ap()
    dr["SW2"] = nc.dram_tensor("SW2", [2 * INTER, D], F32, kind="ExternalInput").ap()
    dr["SW3"] = nc.dram_tensor("SW3", [D, 2 * INTER], F32, kind="ExternalInput").ap()
    dr["EB"] = nc.dram_tensor("EB", [128, NE], F32, kind="ExternalInput").ap()
    dr["HHB"] = nc.dram_tensor("HHB", [128, D], F32, kind="ExternalInput").ap()
    dr["GNW"] = nc.dram_tensor("GNW", [128, D], F32, kind="ExternalInput").ap()
    dr["GNB"] = nc.dram_tensor("GNB", [128, D], F32, kind="ExternalInput").ap()
    dr["OUT"] = nc.dram_tensor("OUT", [T, D], F32, kind="ExternalOutput").ap()
    dr["W1P"] = nc.dram_tensor("W1P", [NE, IC, 128, KC, 128], BF16, kind="Internal").ap()
    dr["W3P"] = nc.dram_tensor("W3P", [NE, IC, 128, KC, 128], BF16, kind="Internal").ap()
    dr["W2P"] = nc.dram_tensor("W2P", [NE, 8, 128, IC, 256], BF16, kind="Internal").ap()
    dr["SW1P"] = nc.dram_tensor("SW1P", [2, IC, 128, KC, 128], BF16, kind="Internal").ap()
    dr["SW3P"] = nc.dram_tensor("SW3P", [2, IC, 128, KC, 128], BF16, kind="Internal").ap()
    dr["SW2P"] = nc.dram_tensor("SW2P", [2, 8, 128, IC, 256], BF16, kind="Internal").ap()

    dbg = {}
    for name in cfg["debug"]:
        shp = {"lnxT": [128 * KC, T], "cw": [128, TC * NE], "scores": [128, TC * NE],
               "xph": [T, D], "hT": [128 * IC, 1024]}[name]
        dbg[name] = nc.dram_tensor("DBG_" + name, shp, F32, kind="ExternalOutput").ap()

    ident_np = np.eye(128, dtype=np.float32).astype(ml_dtypes.bfloat16)
    low16_np = np.broadcast_to((np.arange(16)[:, None] < np.arange(16)[None, :])
                               .astype(np.float32).reshape(1, 256), (128, 256)).copy()
    low4_np = np.broadcast_to((np.arange(4)[:, None] < np.arange(4)[None, :])
                              .astype(np.float32).reshape(1, 16), (128, 16)).copy()
    Cn, SAn, SBn = _rope_tables_np()
    bfc = lambda a: a.astype(ml_dtypes.bfloat16)
    d_ident = nc.inline_tensor(ident_np, "c_ident").ap()
    d_low16 = nc.inline_tensor(low16_np, "c_low16").ap()
    d_low4 = nc.inline_tensor(low4_np, "c_low4").ap()
    d_ropeC = nc.inline_tensor(bfc(Cn), "c_ropeC").ap()
    d_ropeSA = nc.inline_tensor(bfc(SAn), "c_ropeSA").ap()
    d_ropeSB = nc.inline_tensor(bfc(SBn), "c_ropeSB").ap()

    sb = lambda n_, s_, d_: es.enter_context(nc.sbuf_tensor(n_, s_, d_))
    xtok = sb("xtok", [128, TC, D], BF16)
    lnxT = sb("lnxT", [128, KC, T], BF16)
    HTR = sb("HTR", [128, 22528], U8)
    STU = sb("STU", [128, 16384], U8)
    GNR = sb("GNR", [128, 13312], U8)
    stw2 = sb("stw2", [128, 2, IC, 256], BF16)
    hhbt = sb("hhbt", [128, D], BF16)
    gnwt = sb("gnwt", [128, D], BF16)
    gnbt = sb("gnbt", [128, D], BF16)
    silub = sb("silub", [128, 2, 512], BF16)
    cw = sb("cw", [128, TC * NE], F32)
    gwT = sb("gwT", [128, KC, NE], BF16)
    identt = sb("identt", [128, 128], BF16)
    low16t = sb("low16t", [128, 256], F32)
    low4t = sb("low4t", [128, 16], F32)
    ebt = sb("ebt", [128, NE], F32)
    ssum = sb("ssum", [128, TC], F32)
    ssq = sb("ssq", [128, TC], F32)
    meanb = sb("meanb", [128, TC], F32)
    msqb = sb("msqb", [128, TC], F32)
    varb = sb("varb", [128, TC], F32)
    rsb = sb("rsb", [128, TC], F32)
    denb = sb("denb", [128, TC], F32)
    recb = sb("recb", [128, TC], F32)

    def view(region, off, nbytes, dt, a=None):
        ap = region[:, off:off + nbytes].bitcast(dt)
        if a is not None:
            ap = ap.rearrange("p (a b) -> p a b", a=a)
        return ap

    hT = view(HTR, 0, IC * 1024 * 2, BF16, a=IC)
    ETB = view(HTR, 0, 1024, F32)
    D1B = view(HTR, 1024, 1024, F32)
    SCO = view(HTR, 2048, 1024, F32)
    BIA = view(HTR, 3072, 1024, F32)
    GSB = view(HTR, 4096, 256, F32)
    CMP4 = view(HTR, 4352, 64, F32)
    TMP4 = view(HTR, 4416, 64, F32)
    U8B = view(HTR, 4480, 32, F32)
    RKG = view(HTR, 4512, 256, F32)
    GMR = view(HTR, 4768, 1024, F32)
    MZB = view(HTR, 5792, 1024, F32)
    MSKB = view(HTR, 6816, 1024, F32)
    CMP16 = view(HTR, 7840, 1024, F32)
    TMP16 = view(HTR, 8864, 1024, F32)
    T8B = view(HTR, 9888, 512, F32)
    T4B = view(HTR, 10400, 256, F32)
    T2B = view(HTR, 10656, 128, F32)
    RKE = view(HTR, 10784, 1024, F32)
    SELB = view(HTR, 11808, 1024, F32)
    gwf_sb = HTR[0:16, 12832:21024].bitcast(F32)
    gwb_sb = HTR[0:16, 4096:8192].bitcast(BF16)
    ropeCt = view(HTR, 0, 2048, BF16, a=TC)
    ropeSAt = view(HTR, 2048, 2048, BF16, a=TC)
    ropeSBt = view(HTR, 4096, 2048, BF16, a=TC)
    HB1 = view(HTR, 6144, 8192, BF16, a=TC)
    HB2 = view(HTR, 14336, 8192, BF16, a=TC)

    DUMP = view(GNR, 0, 8192, F32)
    scrbf = view(GNR, 8192, 4096, BF16)
    TT8b = view(GNR, 0, 4096, BF16, a=64)
    TT8f = GNR[:, 0:4096].bitcast(F32)
    TT4 = view(GNR, 4096, 4096, F32, a=64)
    qsB = view(GNR, 8192, 4096, F32, a=4)
    PS = view(GNR, 10240, 512, F32)
    PSQ = view(GNR, 10752, 512, F32)
    MB = view(GNR, 11264, 512, F32)
    VARB2 = view(GNR, 11776, 512, F32)
    RSB2 = view(GNR, 12288, 512, F32)
    RTB = view(GNR, 12800, 512, F32)

    P = [es.enter_context(nc.psum_tensor(f"P{k}", [128, 512], F32)) for k in range(8)]
    Pbf = [P[k][:, 0:64].bitcast(BF16) for k in range(8)]

    S = Sched(nc, es)
    for ph in ("P", "A", "B", "C"):
        for kind in ("q", "t", "v", "s", "y"):
            S.sem(kind + ph)

    q, t, v, s = "qP", "tP", "vP", "sP"
    x_ready = S.ev("g", lambda g: g.dma_start(
        xtok[:], dr["X"].rearrange("(c p) d -> p c d", p=128)), inc=q, n=16)
    S.ev("g", lambda g: g.dma_start(ebt[:], dr["EB"]), inc=q, n=16)
    S.ev("g", lambda g: g.dma_start(hhbt[:], dr["HHB"]), inc=q, n=16)
    S.ev("g", lambda g: g.dma_start(gnwt[:], dr["GNW"]), inc=q, n=16)
    S.ev("g", lambda g: g.dma_start(gnbt[:], dr["GNB"]), inc=q, n=16)
    S.ev("g", lambda g: g.dma_start(identt[:], d_ident), inc=q, n=16)
    S.ev("g", lambda g: g.dma_start(low16t[:], d_low16), inc=q, n=16)
    S.ev("g", lambda g: g.dma_start(low4t[:], d_low4), inc=q, n=16)
    gw_dma = S.ev("g", lambda g: g.dma_start(gwf_sb, dr["GW"]), inc=q, n=16)
    gwb_done = S.ev("v", lambda vv: vv.tensor_copy(gwb_sb, gwf_sb), waits=[gw_dma], inc=v)
    pb_free = [None, None]
    for c in range(KC):
        bk = 6 + (c % 2)
        tm = S.ev("t", (lambda c=c, bk=bk: (lambda tt: tt.transpose(
            Pbf[bk][:, 0:16], gwb_sb[:, c * 128:(c + 1) * 128], identt[0:16, 0:16])))(),
            waits=[gwb_done, pb_free[c % 2]], inc=t)
        pb_free[c % 2] = S.ev("s", (lambda c=c, bk=bk: (lambda ss: ss.copy(
            gwT[:, c, :], Pbf[bk][:, 0:16])))(), waits=[tm], inc=s)
    PH = {"P": {k: S.mark(k + "P") for k in "qtvs"}}

    def mm_group(psum_ap, st_fn, mv_fn, nk):
        def emit(tt):
            mm = None
            for k in range(nk):
                mm = tt.matmul(psum_ap, st_fn(k), mv_fn(k), start=(k == 0), stop=(k == nk - 1))
            return mm
        return emit

    def emit_ln(ph, w0):
        q, t, v, s = "q" + ph, "t" + ph, "v" + ph, "s" + ph
        pe = None
        for tau in range(TC):
            pe = S.ev("s", (lambda tau=tau: (lambda ss: ss.activation(
                DUMP, xtok[:, tau, :], AF.Copy, accum_out=ssum[:, tau:tau + 1])))(),
                waits=(list(w0) if tau == 0 else []) + [pe], inc=s)
            pe = S.ev("s", (lambda tau=tau: (lambda ss: ss.activation(
                DUMP, xtok[:, tau, :], AF.Square, accum_out=ssq[:, tau:tau + 1])))(),
                waits=[pe], inc=s)
        stats = S.mark(s)
        e = S.ev("v", lambda vv: vv.tensor_scalar(meanb[:], ssum[:], 1.0 / D, None, ALU.mult),
                 waits=[stats], inc=v)
        e = S.ev("v", lambda vv: vv.tensor_tensor(msqb[:], meanb[:], meanb[:], ALU.mult),
                 waits=[e], inc=v)
        e2 = S.ev("v", lambda vv: vv.tensor_scalar(varb[:], ssq[:], 1.0 / D, 1e-5,
                                                   ALU.mult, ALU.add), waits=[stats], inc=v)
        e = S.ev("v", lambda vv: vv.tensor_tensor(varb[:], varb[:], msqb[:], ALU.subtract),
                 waits=[e, e2], inc=v)
        e = S.ev("s", lambda ss: ss.activation(rsb[:], varb[:], AF.Ln),
                 waits=[e], inc=s)
        rs_done = S.ev("s", lambda ss: ss.activation(rsb[:], rsb[:], AF.Exp, scale=-0.5),
                       waits=[e], inc=s)
        tbank_free = [None, None]
        trans_done = None
        for tau in range(TC):
            vm = S.ev("v", (lambda tau=tau: (lambda vv: vv.tensor_scalar(
                scrbf, xtok[:, tau, :], meanb[:, tau:tau + 1], rsb[:, tau:tau + 1],
                ALU.subtract, ALU.mult)))(), waits=[rs_done, trans_done], inc=v)
            for c in range(KC):
                bk = 6 + (c % 2)
                tm = S.ev("t", (lambda c=c, bk=bk: (lambda tt: tt.transpose(
                    Pbf[bk], scrbf[:, c * 128:(c + 1) * 128], identt[:])))(),
                    waits=[vm, tbank_free[c % 2]], inc=t)
                tbank_free[c % 2] = S.ev("s", (lambda c=c, tau=tau, bk=bk: (lambda ss: ss.copy(
                    lnxT[:, c, tau * 128:(tau + 1) * 128], Pbf[bk])))(), waits=[tm], inc=s)
            trans_done = S.mark(t)
        return S.mark(s), tbank_free

    def ffn_phase(ph, prev, prepacked):
        q, t, v, s = "q" + ph, "t" + ph, "v" + ph, "s" + ph
        w0 = [prev["q"], prev["t"], prev["v"], prev["s"], x_ready]
        ln_done, tbank_free = emit_ln(ph, w0)

        sbank_free = [None, None]
        for tau in range(TC):
            bk = tau % 2
            tm = S.ev("t", (lambda tau=tau, bk=bk: mm_group(
                P[bk][:, 0:16],
                lambda k, tau=tau: lnxT[:, k, tau * 128:(tau + 1) * 128],
                lambda k: gwT[:, k, :], KC))(),
                waits=[ln_done, sbank_free[bk]], inc=t)
            sbank_free[bk] = S.ev("s", (lambda tau=tau, bk=bk: (lambda ss: ss.activation(
                ETB[:, tau * 16:(tau + 1) * 16], P[bk][:, 0:16], AF.Exp, scale=-1.0)))(),
                waits=[tm], inc=s)
        em = S.mark(s)
        e = S.ev("v", lambda vv: vv.tensor_scalar(D1B, ETB, 1.0, None, ALU.add),
                 waits=[em], inc=v)
        e = S.ev("v", lambda vv: vv.reciprocal(SCO, D1B), waits=[e], inc=v)
        e = S.ev("v", lambda vv: vv.tensor_tensor(
            BIA.rearrange("p (a b) -> p a b", a=TC), SCO.rearrange("p (a b) -> p a b", a=TC),
            ebt[:].unsqueeze(1).broadcast_to([128, TC, NE]), ALU.add), waits=[e], inc=v)
        for tau in range(TC):
            for gg in range(4):
                e = S.ev("v", (lambda tau=tau, gg=gg: (lambda vv: vv.reduce_max(
                    GSB[:, tau * 4 + gg:tau * 4 + gg + 1],
                    BIA[:, tau * 16 + gg * 4:tau * 16 + gg * 4 + 4], axis=AX.X)))(),
                    waits=[e], inc=v)
        for tau in range(TC):
            a4 = GSB[:, tau * 4:tau * 4 + 4].unsqueeze(2).broadcast_to([128, 4, 4])
            b4 = GSB[:, tau * 4:tau * 4 + 4].unsqueeze(1).broadcast_to([128, 4, 4])
            c3 = CMP4.rearrange("p (a b) -> p a b", a=4)
            t3v = TMP4.rearrange("p (a b) -> p a b", a=4)
            e = S.ev("v", (lambda a4=a4, b4=b4, c3=c3: (lambda vv: vv.tensor_tensor(
                c3, a4, b4, ALU.is_gt)))(), waits=[e], inc=v)
            e = S.ev("v", (lambda a4=a4, b4=b4, t3v=t3v: (lambda vv: vv.tensor_tensor(
                t3v, a4, b4, ALU.is_equal)))(), inc=v)
            e = S.ev("v", (lambda t3v=t3v: (lambda vv: vv.tensor_tensor(
                t3v, t3v, low4t[:].rearrange("p (a b) -> p a b", a=4), ALU.mult)))(),
                waits=[e], inc=v)
            e = S.ev("v", (lambda c3=c3, t3v=t3v: (lambda vv: vv.tensor_tensor(
                c3, c3, t3v, ALU.add)))(), waits=[e], inc=v)
            e = S.ev("v", (lambda c3=c3: (lambda vv: vv.tensor_tensor(
                U8B.rearrange("p (a b) -> p a b", a=2), c3[:, 0:2, :], c3[:, 2:4, :],
                ALU.add)))(), waits=[e], inc=v)
            e = S.ev("v", (lambda tau=tau: (lambda vv: vv.tensor_tensor(
                RKG[:, tau * 4:tau * 4 + 4], U8B[:, 0:4], U8B[:, 4:8], ALU.add)))(),
                waits=[e], inc=v)
        e = S.ev("v", lambda vv: vv.tensor_single_scalar(RKG, RKG, 2.0, ALU.is_lt),
                 waits=[e], inc=v)
        for tau in range(TC):
            e = S.ev("v", (lambda tau=tau: (lambda vv: vv.tensor_copy(
                GMR[:, tau * 16:(tau + 1) * 16].rearrange("p (a b) -> p a b", a=4),
                RKG[:, tau * 4:tau * 4 + 4].unsqueeze(2).broadcast_to([128, 4, 4]))))(),
                waits=[e], inc=v)
        gmr_done = e
        e = S.ev("v", lambda vv: vv.tensor_scalar(MZB, GMR, 1e30, -1e30, ALU.mult, ALU.add),
                 waits=[gmr_done], inc=v)
        e = S.ev("v", lambda vv: vv.tensor_tensor(MSKB, BIA, GMR, ALU.mult),
                 waits=[e], inc=v)
        e = S.ev("v", lambda vv: vv.tensor_tensor(MSKB, MSKB, MZB, ALU.add),
                 waits=[e], inc=v)
        for tau in range(TC):
            a16 = MSKB[:, tau * 16:(tau + 1) * 16].unsqueeze(2).broadcast_to([128, 16, 16])
            b16 = MSKB[:, tau * 16:(tau + 1) * 16].unsqueeze(1).broadcast_to([128, 16, 16])
            c3 = CMP16.rearrange("p (a b) -> p a b", a=16)
            t3v = TMP16.rearrange("p (a b) -> p a b", a=16)
            e = S.ev("v", (lambda a16=a16, b16=b16, c3=c3: (lambda vv: vv.tensor_tensor(
                c3, a16, b16, ALU.is_gt)))(), waits=[e], inc=v)
            e = S.ev("v", (lambda a16=a16, b16=b16, t3v=t3v: (lambda vv: vv.tensor_tensor(
                t3v, a16, b16, ALU.is_equal)))(), inc=v)
            e = S.ev("v", (lambda t3v=t3v: (lambda vv: vv.tensor_tensor(
                t3v, t3v, low16t[:].rearrange("p (a b) -> p a b", a=16), ALU.mult)))(),
                waits=[e], inc=v)
            e = S.ev("v", (lambda c3=c3, t3v=t3v: (lambda vv: vv.tensor_tensor(
                c3, c3, t3v, ALU.add)))(), waits=[e], inc=v)
            e = S.ev("v", (lambda c3=c3: (lambda vv: vv.tensor_tensor(
                T8B.rearrange("p (a b) -> p a b", a=8), c3[:, 0:8, :], c3[:, 8:16, :],
                ALU.add)))(), waits=[e], inc=v)
            e = S.ev("v", (lambda vv: vv.tensor_tensor(
                T4B.rearrange("p (a b) -> p a b", a=4),
                T8B.rearrange("p (a b) -> p a b", a=8)[:, 0:4, :],
                T8B.rearrange("p (a b) -> p a b", a=8)[:, 4:8, :], ALU.add)),
                waits=[e], inc=v)
            e = S.ev("v", (lambda vv: vv.tensor_tensor(
                T2B.rearrange("p (a b) -> p a b", a=2),
                T4B.rearrange("p (a b) -> p a b", a=4)[:, 0:2, :],
                T4B.rearrange("p (a b) -> p a b", a=4)[:, 2:4, :], ALU.add)),
                waits=[e], inc=v)
            e = S.ev("v", (lambda tau=tau: (lambda vv: vv.tensor_tensor(
                RKE[:, tau * 16:(tau + 1) * 16], T2B[:, 0:16], T2B[:, 16:32], ALU.add)))(),
                waits=[e], inc=v)
        e = S.ev("v", lambda vv: vv.tensor_single_scalar(SELB, RKE, 4.0, ALU.is_lt),
                 waits=[e], inc=v)
        e = S.ev("v", lambda vv: vv.tensor_tensor(SELB, SELB, SCO, ALU.mult),
                 waits=[e], inc=v)
        for tau in range(TC):
            e = S.ev("v", (lambda tau=tau: (lambda vv: vv.reduce_sum(
                denb[:, tau:tau + 1], SELB[:, tau * 16:(tau + 1) * 16], axis=AX.X)))(),
                waits=[e], inc=v)
        e = S.ev("v", lambda vv: vv.reciprocal(recb[:], denb[:]), waits=[e], inc=v)
        cw_done = S.ev("v", lambda vv: vv.scalar_tensor_tensor(
            cw[:].rearrange("p (a b) -> p a b", a=TC),
            SELB.rearrange("p (a b) -> p a b", a=TC), 0.5,
            recb[:].unsqueeze(2).broadcast_to([128, TC, NE]), ALU.mult, ALU.mult),
            waits=[e], inc=v)

        if ph == "A" and "cw" in dbg:
            S.ev("g", lambda g: g.dma_start(dbg["cw"], cw[:]), waits=[cw_done], inc=q, n=16)
        if ph == "A" and "scores" in dbg:
            S.ev("g", lambda g: g.dma_start(dbg["scores"], SCO), waits=[cw_done], inc=q, n=16)
        if ph == "A" and "lnxT" in dbg:
            S.ev("g", lambda g: g.dma_start(
                dbg["lnxT"].rearrange("(p c) t -> p c t", p=128), lnxT[:]),
                waits=[cw_done, ln_done], inc=q, n=16)

        hbank_free = [sbank_free[0], sbank_free[1], None, None]
        ybank_free = [None, None, tbank_free[0], tbank_free[1]]
        st_free = {("1", 0): [], ("1", 1): [], ("3", 0): [], ("3", 1): [],
                   ("2", 0): [], ("2", 1): []}
        silub_free = [None, None]
        first_dma = True

        sources = ([("e", i) for i in range(cfg["n_experts"])]
                   + [("s", p) for p in range(cfg["n_shared"])])
        for (kind, idx) in sources:
            for th in range(2):
                w1m, w3m, w2m = {}, {}, {}
                yq = "y" + ph
                use_pre = prepacked or th == 1

                def emit_w13(i, extra=()):
                    r = i % 2
                    if kind == "e":
                        p1, p3 = dr["W1P"][idx, i], dr["W3P"][idx, i]
                    else:
                        p1, p3 = dr["SW1P"][idx, i], dr["SW3P"][idx, i]
                    if use_pre:
                        w1m[i] = S.ev("y", (lambda p1=p1, r=r: (lambda sy: sy.dma_start(
                            st1v(r), p1)))(),
                            waits=st_free[("1", r)] + list(extra), inc=yq, n=16)
                        w3m[i] = S.ev("y", (lambda p3=p3, r=r: (lambda sy: sy.dma_start(
                            st3v(r), p3)))(),
                            waits=st_free[("3", r)], inc=yq, n=16)
                        return
                    if kind == "e":
                        src1 = dr["W1"][idx, :, i * 128:(i + 1) * 128]
                        src3 = dr["W3"][idx, :, i * 128:(i + 1) * 128]
                    else:
                        col = idx * INTER + i * 128
                        src1 = dr["SW1"][:, col:col + 128]
                        src3 = dr["SW3"][:, col:col + 128]
                    w1m[i] = S.ev("g", (lambda src1=src1, r=r: (lambda g: g.dma_start(
                        st1v(r), src1.rearrange("(c p) i -> p c i", p=128))))(),
                        waits=st_free[("1", r)] + list(extra), inc=q, n=16)
                    w3m[i] = S.ev("g", (lambda src3=src3, r=r: (lambda g: g.dma_start(
                        st3v(r), src3.rearrange("(c p) i -> p c i", p=128))))(),
                        waits=st_free[("3", r)], inc=q, n=16)
                    wb1 = S.ev("y", (lambda p1=p1, r=r: (lambda sy: sy.dma_start(
                        p1, st1v(r))))(), waits=[w1m[i]], inc=yq, n=16)
                    wb3 = S.ev("y", (lambda p3=p3, r=r: (lambda sy: sy.dma_start(
                        p3, st3v(r))))(), waits=[w3m[i]], inc=yq, n=16)
                    wbm[("1", i)] = wb1
                    wbm[("3", i)] = wb3

                def emit_w2(dc):
                    r = dc % 2
                    p2 = dr["W2P"][idx, dc] if kind == "e" else dr["SW2P"][idx, dc]
                    if use_pre:
                        w2m[dc] = S.ev("y", (lambda p2=p2, r=r: (lambda sy: sy.dma_start(
                            stw2[:, r, :, :], p2)))(),
                            waits=st_free[("2", r)], inc=yq, n=16)
                        return
                    if kind == "e":
                        src2 = dr["W2"][idx, :, dc * 256:(dc + 1) * 256]
                    else:
                        src2 = dr["SW2"][idx * INTER:(idx + 1) * INTER, dc * 256:(dc + 1) * 256]
                    w2m[dc] = S.ev("g", (lambda src2=src2, r=r: (lambda g: g.dma_start(
                        stw2[:, r, :, :], src2.rearrange("(c p) d -> p c d", p=128))))(),
                        waits=st_free[("2", r)], inc=q, n=16)
                    wbm[("2", dc)] = S.ev("y", (lambda p2=p2, r=r: (lambda sy: sy.dma_start(
                        p2, stw2[:, r, :, :])))(), waits=[w2m[dc]], inc=yq, n=16)

                wbm = {}
                emit_w13(0, extra=([prev["t"]] if first_dma else []))
                first_dma = False
                emit_w13(1)
                emit_w2(0)
                emit_w2(1)

                for i in range(IC):
                    r = i % 2
                    for ct in range(2):
                        bA, bB = ct, 2 + ct
                        toff = th * 1024 + ct * 512
                        mA = S.ev("t", (lambda r=r, toff=toff, bA=bA: mm_group(
                            P[bA][:, 0:512],
                            lambda k, r=r: st1v(r)[:, k, :],
                            lambda k, toff=toff: lnxT[:, k, toff:toff + 512], KC))(),
                            waits=[w1m[i], ln_done, hbank_free[bA]], inc=t)
                        mB = S.ev("t", (lambda r=r, toff=toff, bB=bB: mm_group(
                            P[bB][:, 0:512],
                            lambda k, r=r: st3v(r)[:, k, :],
                            lambda k, toff=toff: lnxT[:, k, toff:toff + 512], KC))(),
                            waits=[w3m[i], ln_done, hbank_free[bB]], inc=t)
                        sl = ct
                        sA = S.ev("s", (lambda bA=bA, sl=sl: (lambda ss: ss.activation(
                            silub[:, sl, :], P[bA][:, 0:512], AF.Silu)))(),
                            waits=[mA, silub_free[sl]], inc=s)
                        hbank_free[bA] = sA
                        hv_ = S.ev("v", (lambda i=i, ct=ct, sl=sl, bB=bB: (lambda vv: vv.tensor_tensor(
                            hT[:, i, ct * 512:(ct + 1) * 512], silub[:, sl, :], P[bB][:, 0:512],
                            ALU.mult)))(), waits=[sA, mB], inc=v)
                        hbank_free[bB] = hv_
                        silub_free[sl] = hv_
                        if ct == 1:
                            st_free[("1", r)] = [mA] + [x for x in [wbm.get(("1", i))] if x]
                            st_free[("3", r)] = [mB] + [x for x in [wbm.get(("3", i))] if x]
                    if i + 2 < IC:
                        emit_w13(i + 2)
                h_done = S.mark(v)

                for dc in range(8):
                    r = dc % 2
                    last_y = None
                    for tj in range(8):
                        tau = th * 8 + tj
                        bk = 4 + (tj % 4)
                        ym = S.ev("t", (lambda r=r, tj=tj, bk=bk: mm_group(
                            P[bk][:, 0:256],
                            lambda i, tj=tj: hT[:, i, tj * 128:tj * 128 + 128],
                            lambda i, r=r: stw2[:, r, i, :], IC))(),
                            waits=[w2m[dc], h_done, ybank_free[tj % 4]], inc=t)
                        last_y = ym
                        sc = (cw[:, tau * 16 + idx:tau * 16 + idx + 1] if kind == "e" else 0.5)
                        ybank_free[tj % 4] = S.ev("v", (lambda tau=tau, dc=dc, bk=bk, sc=sc:
                            (lambda vv: vv.scalar_tensor_tensor(
                                xtok[:, tau, dc * 256:(dc + 1) * 256], P[bk][:, 0:256], sc,
                                xtok[:, tau, dc * 256:(dc + 1) * 256], ALU.mult, ALU.add)))(),
                            waits=[ym, cw_done], inc=v)
                    st_free[("2", r)] = [last_y] + [x for x in [wbm.get(("2", dc))] if x]
                    if dc + 2 < 8:
                        emit_w2(dc + 2)

        if ph in ("A",) and "hT" in dbg:
            m = {k: S.mark(k + ph) for k in "qtvs"}
            S.ev("g", lambda g: g.dma_start(
                dbg["hT"].rearrange("(p c) t -> p c t", p=128), hT), waits=[m["v"]], inc=q, n=16)
        if ph in ("A",) and "xph" in dbg:
            m = {k: S.mark(k + ph) for k in "qtvs"}
            S.ev("g", lambda g: g.dma_start(
                dbg["xph"].rearrange("(c p) d -> p c d", p=128), xtok[:]),
                waits=[m["v"], m["t"]], inc=q, n=16)
        return {k: S.mark(k + ph) for k in "qtvs"}

    def st1v(r):
        return STU[:, 0:8192].bitcast(BF16).rearrange("p (r c i) -> p r c i", r=2, c=KC)[:, r, :, :]

    def st3v(r):
        return STU[:, 8192:16384].bitcast(BF16).rearrange("p (r c i) -> p r c i", r=2, c=KC)[:, r, :, :]

    def wqv(r):
        return STU[:, 0:16384].bitcast(BF16).rearrange("p (r c i) -> p r c i", r=2, c=KC)[:, r, :, :]

    def attn_phase(prev):
        ph = "B"
        q, t, v, s = "qB", "tB", "vB", "sB"
        w0 = [prev["q"], prev["t"], prev["v"], prev["s"]]
        rt0 = S.ev("g", lambda g: g.dma_start(ropeCt, d_ropeC), waits=w0, inc=q, n=16)
        S.ev("g", lambda g: g.dma_start(ropeSAt, d_ropeSA), inc=q, n=16)
        S.ev("g", lambda g: g.dma_start(ropeSBt, d_ropeSB), inc=q, n=16)
        tab_done = S.mark(q)

        ln_done, tbank_free = emit_ln(ph, w0)

        wq_free = [None, None]
        qbank_free = [None] * 6
        qs_free = [None, None, None, None]
        tp_free = [tbank_free[0], tbank_free[1]]
        h_war = None
        for cb in range(8):
            r = cb % 2
            wqm = S.ev("g", (lambda cb=cb, r=r: (lambda g: g.dma_start(
                wqv(r), dr["WQ"][:, cb * 256:(cb + 1) * 256]
                .rearrange("(c p) d -> p c d", p=128))))(),
                waits=[wq_free[r], tab_done if cb == 0 else None,
                       prev["t"] if cb == 0 else None], inc=q, n=16)
            tanh_marks = []
            pend = []
            def emit_tanh(tau, sr, ev):
                tm = S.ev("s", (lambda sr=sr, tau=tau: (lambda ss: ss.activation(
                    HB1[:, tau, :], qsB[:, sr, :], AF.Tanh)))(),
                    waits=[ev, h_war], inc=s)
                qs_free[sr] = tm
                tanh_marks.append(tm)
            for tau in range(TC):
                bk = tau % 6
                qm = S.ev("t", (lambda tau=tau, r=r, bk=bk: mm_group(
                    P[bk][:, 0:256],
                    lambda k, tau=tau: lnxT[:, k, tau * 128:(tau + 1) * 128],
                    lambda k, r=r: wqv(r)[:, k, :], KC))(),
                    waits=[wqm, ln_done, qbank_free[bk]], inc=t)
                sr = tau % 4
                if cb == 0:
                    cm = S.ev("s", (lambda bk=bk, sr=sr: (lambda ss: ss.copy(
                        qsB[:, sr, :], P[bk][:, 0:256])))(),
                        waits=[qm, qs_free[sr]], inc=s)
                else:
                    cm = S.ev("s", (lambda bk=bk, sr=sr: (lambda ss: ss.activation(
                        qsB[:, sr, :], P[bk][:, 0:256], AF.Relu)))(),
                        waits=[qm, qs_free[sr]], inc=s)
                qbank_free[bk] = cm
                ev = cm
                if cb == 0:
                    ev = S.ev("v", (lambda sr=sr, tau=tau: (lambda vv: vv.tensor_tensor(
                        RTB[:, 0:64], qsB[:, sr, 0:64], ropeCt[:, tau, :], ALU.mult)))(),
                        waits=[ev, rt0], inc=v)
                    ev = S.ev("v", (lambda sr=sr, tau=tau: (lambda vv: vv.tensor_tensor(
                        RTB[:, 64:128], qsB[:, sr, 1:65], ropeSAt[:, tau, :], ALU.mult)))(),
                        inc=v)
                    ev = S.ev("v", (lambda sr=sr, tau=tau: (lambda vv: vv.tensor_tensor(
                        TT8f[:, 0:63], qsB[:, sr, 0:63], ropeSBt[:, tau, 0:63], ALU.mult)))(),
                        inc=v)
                    ev = S.ev("v", (lambda sr=sr: (lambda vv: vv.tensor_tensor(
                        qsB[:, sr, 0:64], RTB[:, 0:64], RTB[:, 64:128], ALU.add)))(),
                        waits=[ev], inc=v)
                    ev = S.ev("v", (lambda sr=sr: (lambda vv: vv.tensor_tensor(
                        qsB[:, sr, 1:64], qsB[:, sr, 1:64], TT8f[:, 0:63], ALU.add)))(),
                        waits=[ev], inc=v)
                    ev = S.ev("v", (lambda sr=sr: (lambda vv: vv.tensor_scalar_max(
                        qsB[:, sr, :], qsB[:, sr, :], 0.0)))(), waits=[ev], inc=v)
                ev = S.ev("v", (lambda sr=sr, cb=cb: (lambda vv: vv.tensor_tensor(
                    qsB[:, sr, :], qsB[:, sr, :], hhbt[:, cb * 256:(cb + 1) * 256],
                    ALU.add)))(), waits=[ev], inc=v)
                pend.append((tau, sr, ev))
                if len(pend) >= 3:
                    emit_tanh(*pend.pop(0))
            while pend:
                emit_tanh(*pend.pop(0))
            hdone = tanh_marks[-1]
            H3 = HB1.rearrange("p a b -> p (a b)").rearrange("p (a b) -> p a b", a=64)
            SQ3 = HB2.rearrange("p a b -> p (a b)").rearrange("p (a b) -> p a b", a=64)
            e = S.ev("v", lambda vv, H3=H3, SQ3=SQ3: vv.tensor_tensor(SQ3, H3, H3, ALU.mult),
                     waits=[hdone], inc=v)

            def tree(src3, dst_ps, e):
                L1 = TT8b
                e = S.ev("v", (lambda src3=src3: (lambda vv: vv.tensor_tensor(
                    L1, src3[:, :, 0:32], src3[:, :, 32:64], ALU.add)))(), waits=[e], inc="vB")
                e = S.ev("v", lambda vv: vv.tensor_tensor(
                    TT4, TT8b[:, :, 0:16], TT8b[:, :, 16:32], ALU.add), waits=[e], inc="vB")
                L3 = TT8f[:, 0:512].rearrange("p (a b) -> p a b", a=64)
                e = S.ev("v", lambda vv, L3=L3: vv.tensor_tensor(
                    L3, TT4[:, :, 0:8], TT4[:, :, 8:16], ALU.add), waits=[e], inc="vB")
                L4 = TT4.rearrange("p a b -> p (a b)")[:, 0:256].rearrange("p (a b) -> p a b", a=64)
                e = S.ev("v", lambda vv, L3=L3, L4=L4: vv.tensor_tensor(
                    L4, L3[:, :, 0:4], L3[:, :, 4:8], ALU.add), waits=[e], inc="vB")
                L5 = TT8f[:, 512:640].rearrange("p (a b) -> p a b", a=64)
                e = S.ev("v", lambda vv, L4=L4, L5=L5: vv.tensor_tensor(
                    L5, L4[:, :, 0:2], L4[:, :, 2:4], ALU.add), waits=[e], inc="vB")
                L5f = TT8f[:, 512:640]
                e = S.ev("v", lambda vv, L5f=L5f, dst_ps=dst_ps: vv.tensor_tensor(
                    dst_ps[:, 0:127], L5f[:, 0:127], L5f[:, 1:128], ALU.add),
                    waits=[e], inc="vB")
                return e

            e = tree(H3, PS, e)
            e = tree(SQ3, PSQ, e)
            e = S.ev("v", lambda vv: vv.tensor_scalar(MB, PS, 1.0 / 64, None, ALU.mult),
                     waits=[e], inc=v)
            e = S.ev("v", lambda vv: vv.tensor_tensor(VARB2, MB, MB, ALU.mult),
                     waits=[e], inc=v)
            e = S.ev("v", lambda vv: vv.scalar_tensor_tensor(
                VARB2, PSQ, 1.0 / 64, VARB2, ALU.mult, ALU.subtract), waits=[e], inc=v)
            e = S.ev("v", lambda vv: vv.tensor_scalar(VARB2, VARB2, 1.0, 1e-5,
                                                      ALU.mult, ALU.add), waits=[e], inc=v)
            e = S.ev("s", lambda ss: ss.activation(RSB2, VARB2, AF.Ln),
                     waits=[e], inc=s)
            e = S.ev("s", lambda ss: ss.activation(RSB2, RSB2, AF.Exp, scale=-0.5),
                     waits=[e], inc=s)
            m_rep = MB.rearrange("p (g w) -> p g w", g=64)[:, :, 0:1].broadcast_to([128, 64, 64])
            rs_rep = RSB2.rearrange("p (g w) -> p g w", g=64)[:, :, 0:1].broadcast_to([128, 64, 64])
            e = S.ev("v", lambda vv, m_rep=m_rep, H3=H3, SQ3=SQ3: vv.scalar_tensor_tensor(
                SQ3, m_rep, -1.0, H3, ALU.mult, ALU.add), waits=[e], inc=v)
            e = S.ev("v", lambda vv, rs_rep=rs_rep, SQ3=SQ3: vv.tensor_tensor(
                SQ3, SQ3, rs_rep, ALU.mult), waits=[e], inc=v)
            gw_rep = (lambda cb=cb: gnwt[:, cb * 256:(cb + 1) * 256]
                      .unsqueeze(1).broadcast_to([128, TC, 256]))()
            gb_rep = (lambda cb=cb: gnbt[:, cb * 256:(cb + 1) * 256]
                      .unsqueeze(1).broadcast_to([128, TC, 256]))()
            e = S.ev("v", lambda vv, gw_rep=gw_rep: vv.tensor_tensor(
                HB1, HB2, gw_rep, ALU.mult), waits=[e], inc=v)
            gdone = S.ev("v", lambda vv, gb_rep=gb_rep: vv.tensor_tensor(
                HB1, HB1, gb_rep, ALU.add), waits=[e], inc=v)
            last_t = None
            for tau in range(TC):
                for j in range(2):
                    bk = 6 + j
                    tm = S.ev("t", (lambda tau=tau, j=j, bk=bk: (lambda tt: tt.transpose(
                        Pbf[bk], HB1[:, tau, j * 128:(j + 1) * 128], identt[:])))(),
                        waits=[gdone, tp_free[j]], inc=t)
                    last_t = tm
                    tp_free[j] = S.ev("v", (lambda tau=tau, j=j, cb=cb, bk=bk:
                        (lambda vv: vv.tensor_tensor(
                            xtok[:, 2 * cb + j, tau * 128:(tau + 1) * 128],
                            xtok[:, 2 * cb + j, tau * 128:(tau + 1) * 128],
                            Pbf[bk], ALU.add)))(), waits=[tm], inc=v)
            h_war = last_t
            wq_free[r] = S.mark(t)
        return {k: S.mark(k + "B") for k in "qtvs"}

    prev = PH["P"]
    if "A" in cfg["phases"]:
        prev = ffn_phase("A", prev, False)
        PH["A"] = prev
    if "B" in cfg["phases"]:
        prev = attn_phase(prev)
        PH["B"] = prev
    if "C" in cfg["phases"]:
        prev = ffn_phase("C", prev, True)
        PH["C"] = prev

    qf = "q" + cfg["phases"][-1]
    out_dma = S.ev("g", lambda g: g.dma_start(
        dr["OUT"].rearrange("(c p) d -> p c d", p=128), xtok[:]),
        waits=[prev["v"], prev["t"], prev["s"]], inc=qf, n=16)

    with nc.Block() as blk:
        @blk.gpsimd
        def _(g):
            S.run(g, "g")

        @blk.tensor
        def _(tt):
            S.run(tt, "t")

        @blk.vector
        def _(vv):
            S.run(vv, "v")

        @blk.scalar
        def _(ss):
            S.run(ss, "s")

        @blk.sync
        def _(sy):
            S.run(sy, "y")
            sy.wait_ge(S.sems[out_dma[0]], out_dma[1])

    return nc, es


def _in_maps(inputs):
    X = np.ascontiguousarray(np.asarray(inputs["L0_attn_X"], dtype=np.float32))
    wq = np.ascontiguousarray(np.asarray(inputs["L0_attn_W_Q"], dtype=np.float32))
    gw = np.ascontiguousarray(np.asarray(inputs["L0_ffn_gate_w"], dtype=np.float32))
    w1 = np.ascontiguousarray(np.asarray(inputs["L0_ffn_w1"], dtype=np.float32))
    w2 = np.ascontiguousarray(np.asarray(inputs["L0_ffn_w2"], dtype=np.float32))
    w3 = np.ascontiguousarray(np.asarray(inputs["L0_ffn_w3"], dtype=np.float32))
    sw1 = np.ascontiguousarray(np.asarray(inputs["L0_ffn_sw1"], dtype=np.float32))
    sw2 = np.ascontiguousarray(np.asarray(inputs["L0_ffn_sw2"], dtype=np.float32))
    sw3 = np.ascontiguousarray(np.asarray(inputs["L0_ffn_sw3"], dtype=np.float32))
    eb = np.broadcast_to(np.asarray(inputs["L0_ffn_expert_bias"], np.float32)
                         .reshape(1, NE), (128, NE)).copy()
    hhb = np.broadcast_to(np.asarray(inputs["L0_attn_hh_b"], np.float32)
                          .reshape(1, D), (128, D)).copy()
    gnw = np.broadcast_to(np.asarray(inputs["L0_attn_gn_weight"], np.float32)
                          .reshape(1, D), (128, D)).copy()
    gnb = np.broadcast_to(np.asarray(inputs["L0_attn_gn_bias"], np.float32)
                          .reshape(1, D), (128, D)).copy()
    maps = []
    for b in range(NCORES):
        maps.append({"X": np.ascontiguousarray(X[b]), "WQ": wq, "GW": gw,
                     "W1": w1, "W2": w2, "W3": w3,
                     "SW1": sw1, "SW2": sw2, "SW3": sw3,
                     "EB": eb, "HHB": hhb, "GNW": gnw, "GNB": gnb})
    return maps


def kernel(**inputs):
    maps = _in_maps(inputs)
    nc, es = _build()
    res = run_bass_kernel_spmd(nc, maps, core_ids=list(range(NCORES)))
    out = np.stack([np.asarray(res.results[b]["OUT"]) for b in range(NCORES)], axis=0)
    es.close()
    return out
